# version 02 (final)
# baseline (speedup 1.0000x reference)
import numpy as np
import concourse.bass as bass
import concourse.mybir as mybir
from concourse.bass_utils import run_bass_kernel_spmd
from ml_dtypes import bfloat16

F32 = mybir.dt.float32
BF16 = mybir.dt.bfloat16
F16 = mybir.dt.float16
I32 = mybir.dt.int32
AF = mybir.ActivationFunctionType
ALU = mybir.AluOpType
AX = mybir.AxisListType

NCORE = 8
TOK = 2048
NT = TOK // 128
D = 4096
KT = D // 128
INV_SQRT2 = 0.7071067811865476


class Plan:
    def __init__(self):
        self.items = []
        self.total = 0

    def add(self, eng, fn, inc=1, wait=True):
        self.items.append((eng, fn, self.total if wait else -1, inc))
        self.total += inc

    def dma(self, out, in_, **kw):
        self.add("gpsimd", (lambda e, o=out, i=in_, kw=kw: e.dma_start(out=o, in_=i, **kw)), inc=16)

    def v(self, meth, *a, **k):
        self.add("vector", (lambda e, m=meth, a=a, k=k: getattr(e, m)(*a, **k)))

    def s(self, meth, *a, **k):
        self.add("scalar", (lambda e, m=meth, a=a, k=k: getattr(e, m)(*a, **k)))

    def t(self, meth, *a, **k):
        self.add("tensor", (lambda e, m=meth, a=a, k=k: getattr(e, m)(*a, **k)))

    def mm_group(self, mms):
        n = len(mms)
        for i, (o, l, r) in enumerate(mms):
            self.add("tensor",
                     (lambda e, o=o, l=l, r=r, st=(i == 0), sp=(i == n - 1):
                      e.matmul(o, l, r, start=st, stop=sp)),
                     inc=(1 if i == n - 1 else 0), wait=(i == 0))


def emit(nc, plan):
    per = {}
    for it in plan.items:
        per.setdefault(it[0], []).append(it)
    total = plan.total
    with nc.semaphore("gsem") as sem, nc.Block() as blk:
        for name, lst in per.items():
            def body(e, lst=lst, name=name):
                last = 0
                for (_, fn, wt, inc) in lst:
                    if wt > last:
                        e.wait_ge(sem, wt)
                        last = wt
                    ins = fn(e)
                    if inc:
                        ins.then_inc(sem, inc)
                if name == "gpsimd":
                    e.wait_ge(sem, total)
            getattr(blk, name)(body)


def dram_ap(h, offset, dims):
    return bass.AP(h, offset, [list(d) for d in dims])


def ln_ops(P, t_in, t_out, m1, v1, big2, eps=1e-5):
    n = t_in.shape[1]
    P.v("reduce_sum", m1[:], t_in[:], axis=AX.X)
    P.v("tensor_scalar_mul", m1[:], m1[:], 1.0 / n)
    P.v("tensor_scalar", t_out[:], t_in[:], m1[:], None, ALU.subtract)
    P.v("tensor_mul", big2[:], t_out[:], t_out[:])
    P.v("reduce_sum", v1[:], big2[:], axis=AX.X)
    P.v("tensor_scalar", v1[:], v1[:], 1.0 / n, eps, ALU.mult, ALU.add)
    P.s("sqrt", v1[:], v1[:])
    P.v("reciprocal", v1[:], v1[:])
    P.v("tensor_scalar", t_out[:], t_out[:], v1[:], None, ALU.mult)


def phase_lnT(P, nc, ctx, x_ap, lnT_d, ln_d=None, x_is_f16=False):
    xt = ctx.enter_context(nc.sbuf_tensor("A_xt", [128, D], F32))
    x16 = ctx.enter_context(nc.sbuf_tensor("A_x16", [128, D], F16)) if x_is_f16 else None
    big2 = ctx.enter_context(nc.sbuf_tensor("A_big2", [128, D], F32))
    lnf = ctx.enter_context(nc.sbuf_tensor("A_lnf", [128, D], F32))
    lnb = ctx.enter_context(nc.sbuf_tensor("A_lnb", [128, D], BF16))
    slab = ctx.enter_context(nc.sbuf_tensor("A_slab", [128, D], BF16))
    m1 = ctx.enter_context(nc.sbuf_tensor("A_m1", [128, 1], F32))
    v1 = ctx.enter_context(nc.sbuf_tensor("A_v1", [128, 1], F32))
    eyeb = ctx.enter_context(nc.sbuf_tensor("A_eyeb", [128, 128], BF16))
    ptr = ctx.enter_context(nc.psum_tensor("A_ptr", [128, 128], BF16))
    P.dma(eyeb[:], ctx.eye_bf_ap[:, :])
    for tt in range(NT):
        if x_is_f16:
            P.dma(x16[:], x_ap[tt * 128:(tt + 1) * 128, :])
            P.v("tensor_copy", xt[:], x16[:])
        else:
            P.dma(xt[:], x_ap[tt * 128:(tt + 1) * 128, :])
        ln_ops(P, xt, lnf, m1, v1, big2)
        if ln_d is not None:
            P.dma(ln_d.ap()[tt * 128:(tt + 1) * 128, :], lnf[:])
        P.v("tensor_copy", lnb[:], lnf[:])
        for kt in range(KT):
            P.t("transpose", ptr[:], lnb[:, kt * 128:(kt + 1) * 128], eyeb[:])
            P.v("tensor_copy", slab[:, kt * 128:(kt + 1) * 128], ptr[:])
        dst = dram_ap(lnT_d, tt * 128, [(TOK, 128), (128 * TOK, KT), (1, 128)])
        src = slab[:].rearrange("p (a b) -> p a b", b=128)
        P.dma(dst, src)


def phase_lnres(P, nc, ctx, x_ap, a_d, out_ap, a_is_f16=False):
    xt = ctx.enter_context(nc.sbuf_tensor("R_xt", [128, D], F32))
    at = ctx.enter_context(nc.sbuf_tensor("R_at", [128, D], F32))
    a16 = ctx.enter_context(nc.sbuf_tensor("R_a16", [128, D], F16)) if a_is_f16 else None
    big2 = ctx.enter_context(nc.sbuf_tensor("R_big2", [128, D], F32))
    lnf = ctx.enter_context(nc.sbuf_tensor("R_lnf", [128, D], F32))
    m1 = ctx.enter_context(nc.sbuf_tensor("R_m1", [128, 1], F32))
    v1 = ctx.enter_context(nc.sbuf_tensor("R_v1", [128, 1], F32))
    for tt in range(NT):
        sl = slice(tt * 128, (tt + 1) * 128)
        if a_is_f16:
            P.dma(a16[:], a_d.ap()[sl, :])
            P.v("tensor_copy", at[:], a16[:])
        else:
            P.dma(at[:], a_d.ap()[sl, :])
        ln_ops(P, at, lnf, m1, v1, big2)
        P.dma(xt[:], x_ap[sl, :])
        P.v("tensor_add", lnf[:], lnf[:], xt[:])
        P.dma(out_ap[sl, :], lnf[:])


class Ctx:
    pass


import contextlib


def build_l0_attn():
    nc = bass.Bass("TRN2", target_bir_lowering=False)
    x = nc.dram_tensor("x", [TOK, D], F32, kind="ExternalInput")
    wt = nc.dram_tensor("wt", [D, D], F32, kind="ExternalInput")
    maskt = nc.dram_tensor("maskt", [D, D], I32, kind="ExternalInput")
    bias = nc.dram_tensor("bias", [D], F32, kind="ExternalInput")
    at128 = nc.dram_tensor("at128", [128, D], F32, kind="ExternalInput")
    a4096 = nc.dram_tensor("a4096", [D, 128], F32, kind="ExternalInput")
    eye_bf = nc.dram_tensor("eye_bf", [128, 128], BF16, kind="ExternalInput")
    eye_f = nc.dram_tensor("eye_f", [128, 128], F32, kind="ExternalInput")
    y = nc.dram_tensor("y", [TOK, D], F32, kind="ExternalOutput")
    lnT_d = nc.dram_tensor("lnT_d", [D, TOK], BF16, kind="Internal")
    attn_d = nc.dram_tensor("attn_d", [TOK, D], F32, kind="Internal")

    P = Plan()
    with contextlib.ExitStack() as top:
        with contextlib.ExitStack() as ctxA:
            ctxA.eye_bf_ap = eye_bf.ap()
            phase_lnT(P, nc, ctxA, x.ap(), lnT_d)
        with contextlib.ExitStack() as cB:
            at_sb = cB.enter_context(nc.sbuf_tensor("at_sb", [128, D], F32))
            a_sb = cB.enter_context(nc.sbuf_tensor("a_sb", [128, D], F32))
            mi = cB.enter_context(nc.sbuf_tensor("mi", [128, 512], I32))
            mf = cB.enter_context(nc.sbuf_tensor("mf", [128, 512], F32))
            s1 = cB.enter_context(nc.sbuf_tensor("s1", [128, D], F32))
            s1t = cB.enter_context(nc.sbuf_tensor("s1t", [128, 128], F32))
            g_sb = cB.enter_context(nc.sbuf_tensor("g_sb", [128, 128], F32))
            rowe = cB.enter_context(nc.sbuf_tensor("rowe", [128, D], F32))
            eyef = cB.enter_context(nc.sbuf_tensor("eyef", [128, 128], F32))
            wq = cB.enter_context(nc.sbuf_tensor("wq", [128, KT * 1024], BF16))
            wtile = cB.enter_context(nc.sbuf_tensor("wtile", [128, 1024], F32))
            lnts = cB.enter_context(nc.sbuf_tensor("lnts", [128, D], BF16))
            bias_b = cB.enter_context(nc.sbuf_tensor("bias_b", [128, D], F32))
            ach = cB.enter_context(nc.sbuf_tensor("ach", [128, 512], F32))
            pm0 = cB.enter_context(nc.psum_tensor("pm0", [128, 512], F32))
            pm1 = cB.enter_context(nc.psum_tensor("pm1", [128, 512], F32))
            ptf = cB.enter_context(nc.psum_tensor("ptf", [128, 128], F32))

            P.dma(at_sb[:], at128.ap()[:, :])
            src = dram_ap(a4096, 0, [(128, 128), (128 * 128, KT), (1, 128)])
            P.dma(a_sb[:].rearrange("p (a b) -> p a b", b=128), src)
            P.dma(eyef[:], eye_f.ap()[:, :])
            P.dma(bias_b[:], dram_ap(bias, 0, [(0, 128), (1, D)]))
            for c8 in range(8):
                for kt in range(KT):
                    P.dma(mi[:], maskt.ap()[kt * 128:(kt + 1) * 128, c8 * 512:(c8 + 1) * 512])
                    P.v("tensor_copy", mf[:], mi[:])
                    P.add("tensor", (lambda e, o=pm0[:, :], l=a_sb[:, kt * 128:(kt + 1) * 128],
                                     r=mf[:], st=(kt == 0), sp=(kt == KT - 1):
                                     e.matmul(o, l, r, start=st, stop=sp)), inc=1)
                P.v("tensor_copy", s1[:, c8 * 512:(c8 + 1) * 512], pm0[:, :])
            for nt in range(KT):
                P.t("transpose", ptf[:], s1[:, nt * 128:(nt + 1) * 128], eyef[:])
                P.v("tensor_copy", s1t[:], ptf[:])
                P.add("tensor", (lambda e, o=pm1[:, 0:128], l=a_sb[:, nt * 128:(nt + 1) * 128],
                                 r=s1t[:], st=(nt == 0), sp=(nt == KT - 1):
                                 e.matmul(o, l, r, start=st, stop=sp)), inc=1)
            P.v("tensor_scalar", g_sb[:], pm1[:, 0:128], 0.5, None, ALU.is_gt)
            for c8 in range(8):
                P.mm_group([(pm0[:, :], g_sb[:], at_sb[:, c8 * 512:(c8 + 1) * 512])])
                P.v("tensor_copy", rowe[:, c8 * 512:(c8 + 1) * 512], pm0[:, :])
            for q in range(4):
                qs = q * 1024
                for kt in range(KT):
                    P.dma(wtile[:], wt.ap()[kt * 128:(kt + 1) * 128, qs:qs + 1024])
                    for c in range(2):
                        P.mm_group([(pm0[:, :], at_sb[:, kt * 128:(kt + 1) * 128],
                                     rowe[:, qs + c * 512:qs + (c + 1) * 512])])
                        P.v("tensor_mul", wq[:, kt * 1024 + c * 512:kt * 1024 + (c + 1) * 512],
                            wtile[:, c * 512:(c + 1) * 512], pm0[:, :])
                for tt in range(NT):
                    src = dram_ap(lnT_d, tt * 128, [(TOK, 128), (128 * TOK, KT), (1, 128)])
                    P.dma(lnts[:].rearrange("p (a b) -> p a b", b=128), src)
                    for c in range(2):
                        P.mm_group([(pm1[:, :], lnts[:, kt * 128:(kt + 1) * 128],
                                     wq[:, kt * 1024 + c * 512:kt * 1024 + (c + 1) * 512])
                                    for kt in range(KT)])
                        P.v("tensor_add", ach[:], pm1[:, :], bias_b[:, qs + c * 512:qs + (c + 1) * 512])
                        P.dma(attn_d.ap()[tt * 128:(tt + 1) * 128, qs + c * 512:qs + (c + 1) * 512], ach[:])
        with contextlib.ExitStack() as cD:
            phase_lnres(P, nc, cD, x.ap(), attn_d, y.ap())
    emit(nc, P)
    return nc


def build_l0_ffn():
    nc = bass.Bass("TRN2", target_bir_lowering=False)
    x = nc.dram_tensor("x", [TOK, D], F32, kind="ExternalInput")
    perm = nc.dram_tensor("perm", [D], I32, kind="ExternalInput")
    up = nc.dram_tensor("up", [16, 256, 1024], F32, kind="ExternalInput")
    down = nc.dram_tensor("down", [16, 1024, 256], F32, kind="ExternalInput")
    iota2d = nc.dram_tensor("iota2d", [128, D], F32, kind="ExternalInput")
    iotacols = nc.dram_tensor("iotacols", [128, KT], F32, kind="ExternalInput")
    eye_bf = nc.dram_tensor("eye_bf", [128, 128], BF16, kind="ExternalInput")
    y = nc.dram_tensor("y", [TOK, D], F32, kind="ExternalOutput")
    lnT_d = nc.dram_tensor("lnT_d", [D, TOK], BF16, kind="Internal")
    p_d = nc.dram_tensor("p_d", [D, D], BF16, kind="Internal")
    pt_d = nc.dram_tensor("pt_d", [D, D], BF16, kind="Internal")
    o_d = nc.dram_tensor("o_d", [TOK, D], F32, kind="Internal")
    permrow_d = nc.dram_tensor("permrow_d", [D], F32, kind="Internal")

    P = Plan()
    with contextlib.ExitStack() as ctxA:
        ctxA.eye_bf_ap = eye_bf.ap()
        phase_lnT(P, nc, ctxA, x.ap(), lnT_d)
    with contextlib.ExitStack() as cB:
        permc = cB.enter_context(nc.sbuf_tensor("permc", [128, 1], I32))
        permcf = cB.enter_context(nc.sbuf_tensor("permcf", [128, 1], F32))
        permr = cB.enter_context(nc.sbuf_tensor("permr", [1, D], I32))
        permrf = cB.enter_context(nc.sbuf_tensor("permrf", [1, D], F32))
        permrb = cB.enter_context(nc.sbuf_tensor("permrb", [128, D], F32))
        io2 = cB.enter_context(nc.sbuf_tensor("io2", [128, D], F32))
        ioc = cB.enter_context(nc.sbuf_tensor("ioc", [128, KT], F32))
        ptile = cB.enter_context(nc.sbuf_tensor("ptile", [128, D], BF16))
        P.dma(io2[:], iota2d.ap()[:, :])
        P.dma(ioc[:], iotacols.ap()[:, :])
        P.dma(permr[:], dram_ap(perm, 0, [(0, 1), (1, D)]))
        P.v("tensor_copy", permrf[:], permr[:])
        P.dma(dram_ap(permrow_d, 0, [(0, 1), (1, D)]), permrf[:])
        P.dma(permrb[:], dram_ap(permrow_d, 0, [(0, 128), (1, D)]))
        for jt in range(KT):
            P.dma(permc[:], dram_ap(perm, jt * 128, [(1, 128), (0, 1)]))
            P.v("tensor_copy", permcf[:], permc[:])
            P.v("tensor_scalar", ptile[:], io2[:], permcf[:], None, ALU.is_equal)
            P.dma(pt_d.ap()[jt * 128:(jt + 1) * 128, :], ptile[:])
        for kt in range(KT):
            P.v("tensor_scalar", ptile[:], permrb[:], ioc[:, kt:kt + 1], None, ALU.is_equal)
            P.dma(p_d.ap()[kt * 128:(kt + 1) * 128, :], ptile[:])
    with contextlib.ExitStack() as cC:
        lnts = cC.enter_context(nc.sbuf_tensor("lnts", [128, KT * 512], BF16))
        pcol = cC.enter_context(nc.sbuf_tensor("pcol", [128, D], BF16))
        tps = cC.enter_context(nc.sbuf_tensor("tps", [128, KT * 512], BF16))
        fsl = cC.enter_context(nc.sbuf_tensor("fsl", [128, KT * 512], BF16))
        pts = cC.enter_context(nc.sbuf_tensor("pts", [128, KT * 512], BF16))
        upf = cC.enter_context(nc.sbuf_tensor("upf", [128, 2 * 1024], F32))
        upb = cC.enter_context(nc.sbuf_tensor("upb", [128, 2 * 1024], BF16))
        dnf = cC.enter_context(nc.sbuf_tensor("dnf", [128, 8 * 256], F32))
        dnb = cC.enter_context(nc.sbuf_tensor("dnb", [128, 8 * 256], BF16))
        gel = cC.enter_context(nc.sbuf_tensor("gel", [128, 8 * 512], BF16))
        erf_t = cC.enter_context(nc.sbuf_tensor("erf_t", [128, 512], F32))
        och = cC.enter_context(nc.sbuf_tensor("och", [128, 512], F32))
        pm0 = cC.enter_context(nc.psum_tensor("pm0", [128, 512], F32))
        pm1 = cC.enter_context(nc.psum_tensor("pm1", [128, 512], F32))
        for ttc in range(4):
            src = dram_ap(lnT_d, ttc * 512, [(TOK, 128), (128 * TOK, KT), (1, 512)])
            P.dma(lnts[:].rearrange("p (a b) -> p a b", b=512), src)
            for jt in range(KT):
                src = dram_ap(p_d, jt * 128, [(D, 128), (128 * D, KT), (1, 128)])
                P.dma(pcol[:].rearrange("p (a b) -> p a b", b=128), src)
                P.mm_group([(pm0[:, :], pcol[:, kt * 128:(kt + 1) * 128],
                             lnts[:, kt * 512:(kt + 1) * 512]) for kt in range(KT)])
                P.v("tensor_copy", tps[:, jt * 512:(jt + 1) * 512], pm0[:, :])
            for h in range(16):
                srcu = dram_ap(up, h * 256 * 1024, [(1024, 128), (128 * 1024, 2), (1, 1024)])
                P.dma(upf[:].rearrange("p (a b) -> p a b", b=1024), srcu)
                P.v("tensor_copy", upb[:], upf[:])
                srcd = dram_ap(down, h * 1024 * 256, [(256, 128), (128 * 256, 8), (1, 256)])
                P.dma(dnf[:].rearrange("p (a b) -> p a b", b=256), srcd)
                P.v("tensor_copy", dnb[:], dnf[:])
                for mc in range(8):
                    P.mm_group([(pm1[:, :], upb[:, dc * 1024 + mc * 128:dc * 1024 + (mc + 1) * 128],
                                 tps[:, (2 * h + dc) * 512:(2 * h + dc + 1) * 512]) for dc in range(2)])
                    P.s("activation", erf_t[:], pm1[:, :], AF.Erf, scale=INV_SQRT2)
                    P.v("tensor_scalar", erf_t[:], erf_t[:], 1.0, 0.5, ALU.add, ALU.mult)
                    P.v("tensor_tensor", gel[:, mc * 512:(mc + 1) * 512], erf_t[:], pm1[:, :], ALU.mult)
                for dc in range(2):
                    P.mm_group([(pm0[:, :], dnb[:, mc * 256 + dc * 128:mc * 256 + (dc + 1) * 128],
                                 gel[:, mc * 512:(mc + 1) * 512]) for mc in range(8)])
                    P.v("tensor_copy", fsl[:, (2 * h + dc) * 512:(2 * h + dc + 1) * 512], pm0[:, :])
            for kc in range(8):
                src = dram_ap(pt_d, kc * 512, [(D, 128), (128 * D, KT), (1, 512)])
                P.dma(pts[:].rearrange("p (a b) -> p a b", b=512), src)
                for cc in range(4):
                    P.mm_group([(pm1[:, :], fsl[:, jt * 512 + cc * 128:jt * 512 + (cc + 1) * 128],
                                 pts[:, jt * 512:(jt + 1) * 512]) for jt in range(KT)])
                    P.v("tensor_copy", och[:], pm1[:, :])
                    row = ttc * 512 + cc * 128
                    P.dma(o_d.ap()[row:row + 128, kc * 512:(kc + 1) * 512], och[:])
    with contextlib.ExitStack() as cD:
        phase_lnres(P, nc, cD, x.ap(), o_d, y.ap())
    emit(nc, P)
    return nc


def build_l1_s4():
    nc = bass.Bass("TRN2", target_bir_lowering=False)
    x = nc.dram_tensor("x", [TOK, D], F32, kind="ExternalInput")
    lam = nc.dram_tensor("lam", [64, 2], F32, kind="ExternalInput")
    pv = nc.dram_tensor("pv", [64, 2], F32, kind="ExternalInput")
    qv = nc.dram_tensor("qv", [64, 2], F32, kind="ExternalInput")
    vc = nc.dram_tensor("vc", [64, 128], F32, kind="ExternalInput")
    ct = nc.dram_tensor("ct", [64, 2], F32, kind="ExternalInput")
    bprow = nc.dram_tensor("bprow", [1, 64], F32, kind="ExternalInput")
    dvec = nc.dram_tensor("dvec", [1], F32, kind="ExternalInput")
    lstep = nc.dram_tensor("lstep", [1], F32, kind="ExternalInput")
    tanc = nc.dram_tensor("tanc", [1, D], F32, kind="ExternalInput")
    cre = nc.dram_tensor("cre", [128, 32], F32, kind="ExternalInput")
    cim = nc.dram_tensor("cim", [128, 32], F32, kind="ExternalInput")
    cosb = nc.dram_tensor("cosb", [D, D], BF16, kind="ExternalInput")
    sinb = nc.dram_tensor("sinb", [D, D], BF16, kind="ExternalInput")
    eye_bf = nc.dram_tensor("eye_bf", [128, 128], BF16, kind="ExternalInput")
    y = nc.dram_tensor("y", [TOK, D], F32, kind="ExternalOutput")
    lnT_d = nc.dram_tensor("lnT_d", [D, TOK], BF16, kind="Internal")
    ln_d = nc.dram_tensor("ln_d", [TOK, D], F32, kind="Internal")
    y_d = nc.dram_tensor("y_d", [TOK, D], F32, kind="Internal")
    scr_d = nc.dram_tensor("scr_d", [16], F32, kind="Internal")
    are_d = nc.dram_tensor("are_d", [D], F32, kind="Internal")
    aim_d = nc.dram_tensor("aim_d", [D], F32, kind="Internal")
    z_d = nc.dram_tensor("z_d", [2 * D], BF16, kind="Internal")
    t_d = nc.dram_tensor("t_d", [D, D], BF16, kind="Internal")
    kscr_d = nc.dram_tensor("kscr_d", [512], F32, kind="Internal")

    P = Plan()
    with contextlib.ExitStack() as ctxA:
        ctxA.eye_bf_ap = eye_bf.ap()
        phase_lnT(P, nc, ctxA, x.ap(), lnT_d, ln_d=ln_d)
    with contextlib.ExitStack() as cB:
        sm = cB.enter_context(nc.sbuf_tensor("sm", [64, 16], F32))
        vcs = cB.enter_context(nc.sbuf_tensor("vcs", [64, 128], F32))
        bpb = cB.enter_context(nc.sbuf_tensor("bpb", [64, 64], F32))
        t64 = cB.enter_context(nc.sbuf_tensor("t64", [64, 64], F32))
        one1 = cB.enter_context(nc.sbuf_tensor("one1", [1, 1], F32))
        s64 = cB.enter_context(nc.sbuf_tensor("s64", [64, 1], F32))
        big = [cB.enter_context(nc.sbuf_tensor(f"bg{i}", [64, D], F32)) for i in range(5)]
        kq = [cB.enter_context(nc.sbuf_tensor(f"kq{i}", [128, 32], F32)) for i in range(8)]
        sc1 = [cB.enter_context(nc.sbuf_tensor(f"sc{i}", [128, 32], F32)) for i in range(6)]
        t512 = cB.enter_context(nc.sbuf_tensor("t512", [1, 512], F32))
        acol = cB.enter_context(nc.sbuf_tensor("acol", [128, 2], F32))
        ctab = cB.enter_context(nc.sbuf_tensor("ctab", [128, 512], BF16))
        stab = cB.enter_context(nc.sbuf_tensor("stab", [128, 512], BF16))
        abf = cB.enter_context(nc.sbuf_tensor("abf", [128, 2], BF16))
        krow = cB.enter_context(nc.sbuf_tensor("krow", [1, D], F32))
        krowb = cB.enter_context(nc.sbuf_tensor("krowb", [1, D], BF16))
        zer = cB.enter_context(nc.sbuf_tensor("zer", [1, D], BF16))
        pk = cB.enter_context(nc.psum_tensor("pk", [128, 512], F32))
        P.dma(sm[:, 0:2], lam.ap()[:, :])
        P.dma(sm[:, 2:4], pv.ap()[:, :])
        P.dma(sm[:, 4:6], qv.ap()[:, :])
        P.dma(sm[:, 6:8], ct.ap()[:, :])
        P.dma(vcs[:], vc.ap()[:, :])
        P.dma(bpb[:], dram_ap(bprow, 0, [(0, 64), (1, 64)]))
        P.v("tensor_mul", t64[:], vcs[:].rearrange("p (m two) -> p m two", two=2)[:, :, 0:1].squeeze(2), bpb[:])
        P.v("reduce_sum", sm[:, 8:9], t64[:], axis=AX.X)
        P.v("tensor_mul", t64[:], vcs[:].rearrange("p (m two) -> p m two", two=2)[:, :, 1:2].squeeze(2), bpb[:])
        P.v("reduce_sum", sm[:, 9:10], t64[:], axis=AX.X)

        def cmul_conj(ar, ai, br, bi, outr, outi, tmp):
            P.v("tensor_mul", outr, ar, br)
            P.v("tensor_mul", tmp, ai, bi)
            P.v("tensor_add", outr, outr, tmp)
            P.v("tensor_mul", outi, ar, bi)
            P.v("tensor_mul", tmp, ai, br)
            P.v("tensor_sub", outi, outi, tmp)
        vv = cB.enter_context(nc.sbuf_tensor("vv", [64, 12], F32))
        tmp1 = cB.enter_context(nc.sbuf_tensor("tmp1", [64, 1], F32))
        col = lambda t, i: t[:, i:i + 1]
        cmul_conj(col(sm, 6), col(sm, 7), col(sm, 8), col(sm, 9), col(vv, 0), col(vv, 1), tmp1[:])
        cmul_conj(col(sm, 6), col(sm, 7), col(sm, 2), col(sm, 3), col(vv, 2), col(vv, 3), tmp1[:])
        cmul_conj(col(sm, 4), col(sm, 5), col(sm, 8), col(sm, 9), col(vv, 4), col(vv, 5), tmp1[:])
        cmul_conj(col(sm, 4), col(sm, 5), col(sm, 2), col(sm, 3), col(vv, 6), col(vv, 7), tmp1[:])
        for i, srcc in enumerate([1, 3, 5, 7]):
            P.v("tensor_scalar_mul", col(vv, 8 + i), col(vv, srcc), -1.0)
        P.dma(one1[:], dram_ap(lstep, 0, [(0, 1), (1, 1)]))
        P.s("activation", one1[:], one1[:], AF.Exp)
        P.dma(dram_ap(scr_d, 0, [(0, 1), (1, 1)]), one1[:])
        P.dma(s64[:], dram_ap(scr_d, 0, [(0, 64), (0, 1)]))
        P.v("reciprocal", s64[:], s64[:])
        P.v("tensor_scalar_mul", s64[:], s64[:], 2.0)
        P.dma(big[0][:], dram_ap(tanc, 0, [(0, 64), (1, D)]))
        P.v("tensor_scalar", big[0][:], big[0][:], s64[:], None, ALU.mult)
        P.v("tensor_scalar", big[0][:], big[0][:], col(sm, 1), None, ALU.subtract)
        P.v("memset", big[1][:], 0.0)
        P.v("tensor_scalar", big[1][:], big[1][:], col(sm, 0), None, ALU.subtract)
        P.v("tensor_mul", big[2][:], big[0][:], big[0][:])
        P.v("tensor_mul", big[3][:], big[1][:], big[1][:])
        P.v("tensor_add", big[2][:], big[2][:], big[3][:])
        P.v("reciprocal", big[2][:], big[2][:])
        P.v("tensor_mul", big[3][:], big[1][:], big[2][:])
        P.v("tensor_mul", big[4][:], big[0][:], big[2][:])
        P.v("tensor_scalar_mul", big[4][:], big[4][:], -1.0)
        for qi in range(4):
            vr, vi, vin = col(vv, 2 * qi), col(vv, 2 * qi + 1), col(vv, 8 + qi)
            for c8 in range(8):
                rr = slice(c8 * 512, (c8 + 1) * 512)
                P.mm_group([(pk[0:1, :], vr, big[3][:, rr]), (pk[0:1, :], vin, big[4][:, rr])])
                P.v("tensor_copy", t512[:], pk[0:1, :])
                P.dma(dram_ap(kscr_d, 0, [(0, 1), (1, 512)]), t512[:])
                P.dma(kq[2 * qi][16 * c8:16 * c8 + 16, :], dram_ap(kscr_d, 0, [(32, 16), (1, 32)]))
                P.mm_group([(pk[0:1, :], vr, big[4][:, rr]), (pk[0:1, :], vi, big[3][:, rr])])
                P.v("tensor_copy", t512[:], pk[0:1, :])
                P.dma(dram_ap(kscr_d, 0, [(0, 1), (1, 512)]), t512[:])
                P.dma(kq[2 * qi + 1][16 * c8:16 * c8 + 16, :], dram_ap(kscr_d, 0, [(32, 16), (1, 32)]))
        P.v("tensor_scalar_add", sc1[0][:], kq[6][:], 1.0)
        P.v("tensor_mul", sc1[1][:], sc1[0][:], sc1[0][:])
        P.v("tensor_mul", sc1[2][:], kq[7][:], kq[7][:])
        P.v("tensor_add", sc1[1][:], sc1[1][:], sc1[2][:])
        P.v("reciprocal", sc1[1][:], sc1[1][:])
        P.v("tensor_mul", sc1[2][:], sc1[0][:], sc1[1][:])
        P.v("tensor_mul", sc1[3][:], kq[7][:], sc1[1][:])
        P.v("tensor_scalar_mul", sc1[3][:], sc1[3][:], -1.0)
        P.v("tensor_mul", sc1[0][:], kq[2][:], sc1[2][:])
        P.v("tensor_mul", sc1[1][:], kq[3][:], sc1[3][:])
        P.v("tensor_sub", sc1[0][:], sc1[0][:], sc1[1][:])
        P.v("tensor_mul", sc1[1][:], kq[2][:], sc1[3][:])
        P.v("tensor_mul", sc1[4][:], kq[3][:], sc1[2][:])
        P.v("tensor_add", sc1[1][:], sc1[1][:], sc1[4][:])
        P.v("tensor_mul", sc1[2][:], sc1[0][:], kq[4][:])
        P.v("tensor_mul", sc1[3][:], sc1[1][:], kq[5][:])
        P.v("tensor_sub", sc1[2][:], sc1[2][:], sc1[3][:])
        P.v("tensor_mul", sc1[3][:], sc1[0][:], kq[5][:])
        P.v("tensor_mul", sc1[4][:], sc1[1][:], kq[4][:])
        P.v("tensor_add", sc1[3][:], sc1[3][:], sc1[4][:])
        P.v("tensor_sub", sc1[0][:], kq[0][:], sc1[2][:])
        P.v("tensor_sub", sc1[1][:], kq[1][:], sc1[3][:])
        P.dma(sc1[4][:], cre.ap()[:, :])
        P.dma(sc1[5][:], cim.ap()[:, :])
        P.v("tensor_mul", sc1[2][:], sc1[4][:], sc1[0][:])
        P.v("tensor_mul", tmp2 := sc1[3][:], sc1[5][:], sc1[1][:])
        P.v("tensor_sub", kq[0][:], sc1[2][:], sc1[3][:])
        P.v("tensor_mul", sc1[2][:], sc1[4][:], sc1[1][:])
        P.v("tensor_mul", sc1[3][:], sc1[5][:], sc1[0][:])
        P.v("tensor_add", kq[1][:], sc1[2][:], sc1[3][:])
        P.v("tensor_scalar_mul", kq[1][:], kq[1][:], -1.0)
        P.dma(dram_ap(are_d, 0, [(32, 128), (1, 32)]), kq[0][:])
        P.dma(dram_ap(aim_d, 0, [(32, 128), (1, 32)]), kq[1][:])
        for c8 in range(8):
            for kt in range(KT):
                P.dma(acol[:, 0:1], dram_ap(are_d, kt * 128, [(1, 128), (0, 1)]))
                P.dma(acol[:, 1:2], dram_ap(aim_d, kt * 128, [(1, 128), (0, 1)]))
                P.v("tensor_copy", abf[:], acol[:])
                P.dma(ctab[:], cosb.ap()[kt * 128:(kt + 1) * 128, c8 * 512:(c8 + 1) * 512])
                P.dma(stab[:], sinb.ap()[kt * 128:(kt + 1) * 128, c8 * 512:(c8 + 1) * 512])
                P.add("tensor", (lambda e, o=pk[0:1, :], l=abf[:, 0:1], r=ctab[:], st=(kt == 0):
                                 e.matmul(o, l, r, start=st, stop=False)), inc=1)
                P.add("tensor", (lambda e, o=pk[0:1, :], l=abf[:, 1:2], r=stab[:], sp=(kt == KT - 1):
                                 e.matmul(o, l, r, start=False, stop=sp)), inc=1)
            P.v("tensor_copy", krow[0:1, c8 * 512:(c8 + 1) * 512], pk[0:1, :])
        P.v("tensor_scalar_mul", krow[:], krow[:], 1.0 / D)
        P.v("tensor_copy", krowb[:], krow[:])
        P.v("memset", zer[:], 0.0)
        P.dma(dram_ap(z_d, 0, [(0, 1), (1, D)]), zer[:])
        P.dma(dram_ap(z_d, D, [(0, 1), (1, D)]), krowb[:])
        for j in range(D):
            P.add("gpsimd", (lambda e, d=dram_ap(t_d, j * D, [(0, 1), (1, D)]),
                             s=dram_ap(z_d, D - j, [(0, 1), (1, D)]):
                             e.dma_start(out=d, in_=s)), inc=16, wait=(j == 0))
    with contextlib.ExitStack() as cC:
        ts = cC.enter_context(nc.sbuf_tensor("ts", [128, KT * 512], BF16))
        uts = cC.enter_context(nc.sbuf_tensor("uts", [128, KT * 128], BF16))
        uch = cC.enter_context(nc.sbuf_tensor("uch", [128, 512], F32))
        ych = cC.enter_context(nc.sbuf_tensor("ych", [128, 512], F32))
        db = cC.enter_context(nc.sbuf_tensor("db", [128, 1], F32))
        pm1 = cC.enter_context(nc.psum_tensor("pm1c", [128, 512], F32))
        P.dma(db[:], dram_ap(dvec, 0, [(0, 128), (0, 1)]))
        for tc in range(8):
            for jt in range(KT):
                src = dram_ap(t_d, jt * 128 * D + tc * 512, [(D, 128), (1, 512)])
                P.dma(ts[:, jt * 512:(jt + 1) * 512], src)
            for tt in range(NT):
                src = dram_ap(lnT_d, tt * 128, [(TOK, 128), (128 * TOK, KT), (1, 128)])
                P.dma(uts[:].rearrange("p (a b) -> p a b", b=128), src)
                P.mm_group([(pm1[:, :], uts[:, jt * 128:(jt + 1) * 128],
                             ts[:, jt * 512:(jt + 1) * 512]) for jt in range(KT)])
                P.dma(uch[:], ln_d.ap()[tt * 128:(tt + 1) * 128, tc * 512:(tc + 1) * 512])
                P.v("scalar_tensor_tensor", ych[:], uch[:], db[:], pm1[:, :], ALU.mult, ALU.add)
                P.dma(y_d.ap()[tt * 128:(tt + 1) * 128, tc * 512:(tc + 1) * 512], ych[:])
    with contextlib.ExitStack() as cD:
        phase_lnres(P, nc, cD, x.ap(), y_d, y.ap())
    emit(nc, P)
    return nc


def build_l1_dequant():
    NI = 11008 // 8
    ND = 4096 // 8
    nc = bass.Bass("TRN2", target_bir_lowering=False)
    gq = nc.dram_tensor("gq", [512, NI], I32, kind="ExternalInput")
    gs = nc.dram_tensor("gs", [32, NI], F16, kind="ExternalInput")
    gz = nc.dram_tensor("gz", [32, NI // 8], I32, kind="ExternalInput")
    uq = nc.dram_tensor("uq", [512, NI], I32, kind="ExternalInput")
    us = nc.dram_tensor("us", [32, NI], F16, kind="ExternalInput")
    uz = nc.dram_tensor("uz", [32, NI // 8], I32, kind="ExternalInput")
    dq = nc.dram_tensor("dq", [1376, ND], I32, kind="ExternalInput")
    ds = nc.dram_tensor("ds", [86, ND], F16, kind="ExternalInput")
    dz = nc.dram_tensor("dz", [86, ND // 8], I32, kind="ExternalInput")
    sh128 = nc.dram_tensor("sh128", [128, 1], I32, kind="ExternalInput")
    sh8 = nc.dram_tensor("sh8", [8, 1], I32, kind="ExternalInput")
    wg = nc.dram_tensor("wg", [4096, NI], BF16, kind="ExternalOutput")
    wu = nc.dram_tensor("wu", [4096, NI], BF16, kind="ExternalOutput")
    wd = nc.dram_tensor("wd", [11008, ND], BF16, kind="ExternalOutput")
    zrow_d = nc.dram_tensor("zrow_d", [NI], F32, kind="Internal")

    P = Plan()
    with contextlib.ExitStack() as cA:
        sh128s = cA.enter_context(nc.sbuf_tensor("sh128s", [128, 1], I32))
        sh8s = cA.enter_context(nc.sbuf_tensor("sh8s", [8, 1], I32))
        P.dma(sh128s[:], sh128.ap()[:, :])
        P.dma(sh8s[:], sh8.ap()[:, :])

        def dequant(qh, sch, zh, outh, nk, ncols, ngr):
            z8 = cA.enter_context(nc.sbuf_tensor(f"z8{outh.name}", [8, ncols // 8], I32))
            zn8 = cA.enter_context(nc.sbuf_tensor(f"zn8{outh.name}", [8, ncols // 8], I32))
            zf = cA.enter_context(nc.sbuf_tensor(f"zf{outh.name}", [8, ncols // 8], F32))
            z1b = cA.enter_context(nc.sbuf_tensor(f"z1{outh.name}", [128, ncols], F32))
            qt = cA.enter_context(nc.sbuf_tensor(f"qt{outh.name}", [128, ncols], I32))
            qn = cA.enter_context(nc.sbuf_tensor(f"qn{outh.name}", [128, ncols], I32))
            qf = cA.enter_context(nc.sbuf_tensor(f"qf{outh.name}", [128, ncols], F32))
            s16 = cA.enter_context(nc.sbuf_tensor(f"s16{outh.name}", [128, ncols], F16))
            s32 = cA.enter_context(nc.sbuf_tensor(f"s32{outh.name}", [128, ncols], F32))
            wb = cA.enter_context(nc.sbuf_tensor(f"wb{outh.name}", [128, ncols], BF16))
            for kt in range(nk):
                P.dma(z8[:], dram_ap(zh, kt * (ncols // 8), [(0, 8), (1, ncols // 8)]))
                P.v("tensor_scalar", zn8[:], z8[:], sh8s[:], 15, ALU.arith_shift_right, ALU.bitwise_and)
                P.v("tensor_scalar_add", zf[:], zn8[:], 1.0)
                P.dma(dram_ap(zrow_d, 0, [(1, 8), (8, ncols // 8)]), zf[:], allow_slow_non_contiguous=True)
                P.dma(z1b[:], dram_ap(zrow_d, 0, [(0, 128), (1, ncols)]))
                for s in range(8):
                    P.dma(qt[16 * s:16 * s + 16, :],
                          dram_ap(qh, 16 * kt * ncols, [(ncols, 16), (1, ncols)]))
                P.v("tensor_scalar", qn[:], qt[:], sh128s[:], 15, ALU.arith_shift_right, ALU.bitwise_and)
                P.v("tensor_copy", qf[:], qn[:])
                P.dma(s16[:], dram_ap(sch, kt * ncols, [(0, 128), (1, ncols)]))
                P.v("tensor_copy", s32[:], s16[:])
                P.v("tensor_sub", qf[:], qf[:], z1b[:])
                P.v("tensor_tensor", wb[:], qf[:], s32[:], ALU.mult)
                for s in range(8):
                    P.dma(dram_ap(outh, (128 * kt + s) * ncols, [(8 * ncols, 16), (1, ncols)]),
                          wb[16 * s:16 * s + 16, :])
        dequant(gq, gs, gz, wg, 32, NI, 32)
        dequant(uq, us, uz, wu, 32, NI, 32)
        dequant(dq, ds, dz, wd, 86, ND, 86)
    emit(nc, P)
    return nc


def build_l1_ffn():
    NI = 11008
    NS = NI // 128
    nc = bass.Bass("TRN2", target_bir_lowering=False)
    x = nc.dram_tensor("x", [TOK, D], F32, kind="ExternalInput")
    wg = nc.dram_tensor("wg", [D, NI], BF16, kind="ExternalInput")
    wu = nc.dram_tensor("wu", [D, NI], BF16, kind="ExternalInput")
    wd = nc.dram_tensor("wd", [NI, D], BF16, kind="ExternalInput")
    eye_bf = nc.dram_tensor("eye_bf", [128, 128], BF16, kind="ExternalInput")
    eye_f = nc.dram_tensor("eye_f", [128, 128], F32, kind="ExternalInput")
    y = nc.dram_tensor("y", [TOK, D], F32, kind="ExternalOutput")
    lnT_d = nc.dram_tensor("lnT_d", [D, TOK], BF16, kind="Internal")
    oT_d = nc.dram_tensor("oT_d", [D, TOK], F32, kind="Internal")

    P = Plan()
    with contextlib.ExitStack() as ctxA:
        ctxA.eye_bf_ap = eye_bf.ap()
        phase_lnT(P, nc, ctxA, x.ap(), lnT_d)
    with contextlib.ExitStack() as cC:
        lnts = cC.enter_context(nc.sbuf_tensor("lnts", [128, KT * 512], BF16))
        wgs = cC.enter_context(nc.sbuf_tensor("wgs", [128, KT * 128], BF16))
        wus = cC.enter_context(nc.sbuf_tensor("wus", [128, KT * 128], BF16))
        inters = cC.enter_context(nc.sbuf_tensor("inters", [128, NS * 512], BF16))
        wds = cC.enter_context(nc.sbuf_tensor("wds", [128, NS * 128], BF16))
        sil = cC.enter_context(nc.sbuf_tensor("sil", [128, 512], F32))
        o16 = cC.enter_context(nc.sbuf_tensor("o16", [128, 512], F16))
        of32 = cC.enter_context(nc.sbuf_tensor("of32", [128, 512], F32))
        pm0 = cC.enter_context(nc.psum_tensor("pm0", [128, 512], F32))
        pm1 = cC.enter_context(nc.psum_tensor("pm1", [128, 512], F32))
        pm2 = cC.enter_context(nc.psum_tensor("pm2", [128, 512], F32))
        for ttc in range(4):
            src = dram_ap(lnT_d, ttc * 512, [(TOK, 128), (128 * TOK, KT), (1, 512)])
            P.dma(lnts[:].rearrange("p (a b) -> p a b", b=512), src)
            for ns in range(NS):
                src = dram_ap(wg, ns * 128, [(NI, 128), (128 * NI, KT), (1, 128)])
                P.dma(wgs[:].rearrange("p (a b) -> p a b", b=128), src)
                src = dram_ap(wu, ns * 128, [(NI, 128), (128 * NI, KT), (1, 128)])
                P.dma(wus[:].rearrange("p (a b) -> p a b", b=128), src)
                P.mm_group([(pm0[:, :], wgs[:, kt * 128:(kt + 1) * 128],
                             lnts[:, kt * 512:(kt + 1) * 512]) for kt in range(KT)])
                P.mm_group([(pm1[:, :], wus[:, kt * 128:(kt + 1) * 128],
                             lnts[:, kt * 512:(kt + 1) * 512]) for kt in range(KT)])
                P.s("activation", sil[:], pm0[:, :], AF.Silu)
                P.v("tensor_tensor", inters[:, ns * 512:(ns + 1) * 512], sil[:], pm1[:, :], ALU.mult)
            for hc in range(KT):
                src = dram_ap(wd, hc * 128, [(D, 128), (128 * D, NS), (1, 128)])
                P.dma(wds[:].rearrange("p (a b) -> p a b", b=128), src)
                P.mm_group([(pm2[:, :], wds[:, ns * 128:(ns + 1) * 128],
                             inters[:, ns * 512:(ns + 1) * 512]) for ns in range(NS)])
                P.s("copy", o16[:], pm2[:, :])
                P.v("tensor_copy", of32[:], o16[:])
                P.dma(oT_d.ap()[hc * 128:(hc + 1) * 128, ttc * 512:(ttc + 1) * 512], of32[:])
    with contextlib.ExitStack() as cD:
        xt = cD.enter_context(nc.sbuf_tensor("R_xt", [128, D], F32))
        at = cD.enter_context(nc.sbuf_tensor("R_at", [128, D], F32))
        osl = cD.enter_context(nc.sbuf_tensor("R_osl", [128, D], F32))
        big2 = cD.enter_context(nc.sbuf_tensor("R_big2", [128, D], F32))
        lnf = cD.enter_context(nc.sbuf_tensor("R_lnf", [128, D], F32))
        m1 = cD.enter_context(nc.sbuf_tensor("R_m1", [128, 1], F32))
        v1 = cD.enter_context(nc.sbuf_tensor("R_v1", [128, 1], F32))
        eyef = cD.enter_context(nc.sbuf_tensor("R_eyef", [128, 128], F32))
        ptf = cD.enter_context(nc.psum_tensor("R_ptf", [128, 128], F32))
        P.dma(eyef[:], eye_f.ap()[:, :])
        for tt in range(NT):
            src = dram_ap(oT_d, tt * 128, [(TOK, 128), (128 * TOK, KT), (1, 128)])
            P.dma(osl[:].rearrange("p (a b) -> p a b", b=128), src)
            for kt in range(KT):
                P.t("transpose", ptf[:], osl[:, kt * 128:(kt + 1) * 128], eyef[:])
                P.v("tensor_copy", at[:, kt * 128:(kt + 1) * 128], ptf[:])
            ln_ops(P, at, lnf, m1, v1, big2)
            P.dma(xt[:], x.ap()[tt * 128:(tt + 1) * 128, :])
            P.v("tensor_add", lnf[:], lnf[:], xt[:])
            P.dma(y.ap()[tt * 128:(tt + 1) * 128, :], lnf[:])
    emit(nc, P)
    return nc


def _consts():
    c = {}
    c["eye_bf"] = np.eye(128, dtype=bfloat16)
    c["eye_f"] = np.eye(128, dtype=np.float32)
    at = np.zeros((128, D), dtype=np.float32)
    at[np.arange(D) // 32, np.arange(D)] = 1.0
    c["at128"] = at
    c["a4096"] = np.ascontiguousarray(at.T)
    c["iota2d"] = np.tile(np.arange(D, dtype=np.float32)[None, :], (128, 1))
    c["iotacols"] = (np.arange(KT, dtype=np.float32)[None, :] * 128
                     + np.arange(128, dtype=np.float32)[:, None])
    c["sh128"] = (4 * (np.arange(128, dtype=np.int32) // 16)).reshape(128, 1)
    c["sh8"] = (4 * np.arange(8, dtype=np.int32)).reshape(8, 1)
    k = np.arange(D, dtype=np.float64)
    c["tanc"] = np.tan(np.pi * k / D).astype(np.float32).reshape(1, D)
    om = np.exp(-2j * np.pi * k / D)
    cc = 2.0 / (1.0 + om)
    c["cre"] = np.real(cc).astype(np.float32).reshape(128, 32)
    c["cim"] = np.imag(cc).astype(np.float32).reshape(128, 32)
    t = np.arange(D, dtype=np.float64)
    ang = 2.0 * np.pi * np.outer(k, t) / D
    c["cosb"] = np.cos(ang).astype(bfloat16)
    c["sinb"] = np.sin(ang).astype(bfloat16)
    return c


_NCS = {}


def _get_nc(name, builder):
    if name not in _NCS:
        _NCS[name] = builder()
    return _NCS[name]


def kernel(**inp):
    inp = {k: np.asarray(v) for k, v in inp.items()}
    C = _consts()
    cores = list(range(NCORE))
    x = np.ascontiguousarray(inp["L0_attn_data"].reshape(16384, D).astype(np.float32))
    shards = [np.ascontiguousarray(x[i * TOK:(i + 1) * TOK]) for i in range(NCORE)]

    wt = np.ascontiguousarray(inp["L0_attn_weight"].T)
    mt = np.ascontiguousarray(inp["L0_attn_mask"].T)
    nc1 = _get_nc("l0a", build_l0_attn)
    maps = [{"x": shards[i], "wt": wt, "maskt": mt, "bias": inp["L0_attn_bias"],
             "at128": C["at128"], "a4096": C["a4096"], "eye_bf": C["eye_bf"],
             "eye_f": C["eye_f"]} for i in range(NCORE)]
    r = run_bass_kernel_spmd(nc1, maps, core_ids=cores)
    x1 = [np.asarray(o["y"]) for o in r.results]

    nc2 = _get_nc("l0f", build_l0_ffn)
    maps = [{"x": x1[i], "perm": inp["L0_ffn_perm"].astype(np.int32),
             "up": inp["L0_ffn_up"], "down": inp["L0_ffn_down"],
             "iota2d": C["iota2d"], "iotacols": C["iotacols"], "eye_bf": C["eye_bf"]}
            for i in range(NCORE)]
    r = run_bass_kernel_spmd(nc2, maps, core_ids=cores)
    x2 = [np.asarray(o["y"]) for o in r.results]

    nc3 = _get_nc("l1s", build_l1_s4)
    maps = [{"x": x2[i], "lam": inp["L1_attn_Lambda_ri"], "pv": inp["L1_attn_p_ri"],
             "qv": inp["L1_attn_q_ri"], "vc": inp["L1_attn_Vc_ri"].reshape(64, 128),
             "ct": inp["L1_attn_Ct_ri"].reshape(64, 2),
             "bprow": inp["L1_attn_Bp"].reshape(1, 64), "dvec": inp["L1_attn_D"],
             "lstep": inp["L1_attn_log_step"], "tanc": C["tanc"], "cre": C["cre"],
             "cim": C["cim"], "cosb": C["cosb"], "sinb": C["sinb"], "eye_bf": C["eye_bf"]}
            for i in range(NCORE)]
    r = run_bass_kernel_spmd(nc3, maps, core_ids=cores)
    x3 = [np.asarray(o["y"]) for o in r.results]

    NIc = 11008 // 8
    NDc = 4096 // 8
    nc4 = _get_nc("l1d", build_l1_dequant)
    maps = []
    for i in range(NCORE):
        si = slice(i * NIc, (i + 1) * NIc)
        sd = slice(i * NDc, (i + 1) * NDc)
        maps.append({
            "gq": np.ascontiguousarray(inp["L1_ffn_gate_qweight"][:, si]),
            "gs": np.ascontiguousarray(inp["L1_ffn_gate_scales"][:, si]),
            "gz": np.ascontiguousarray(inp["L1_ffn_gate_zeros"][:, i * NIc // 8:(i + 1) * NIc // 8]),
            "uq": np.ascontiguousarray(inp["L1_ffn_up_qweight"][:, si]),
            "us": np.ascontiguousarray(inp["L1_ffn_up_scales"][:, si]),
            "uz": np.ascontiguousarray(inp["L1_ffn_up_zeros"][:, i * NIc // 8:(i + 1) * NIc // 8]),
            "dq": np.ascontiguousarray(inp["L1_ffn_down_qweight"][:, sd]),
            "ds": np.ascontiguousarray(inp["L1_ffn_down_scales"][:, sd]),
            "dz": np.ascontiguousarray(inp["L1_ffn_down_zeros"][:, i * NDc // 8:(i + 1) * NDc // 8]),
            "sh128": C["sh128"], "sh8": C["sh8"]})
    r = run_bass_kernel_spmd(nc4, maps, core_ids=cores)
    wg = np.concatenate([np.asarray(o["wg"]) for o in r.results], axis=1)
    wu = np.concatenate([np.asarray(o["wu"]) for o in r.results], axis=1)
    wd = np.concatenate([np.asarray(o["wd"]) for o in r.results], axis=1)

    nc5 = _get_nc("l1f", build_l1_ffn)
    maps = [{"x": x3[i], "wg": wg, "wu": wu, "wd": wd, "eye_bf": C["eye_bf"],
             "eye_f": C["eye_f"]} for i in range(NCORE)]
    r = run_bass_kernel_spmd(nc5, maps, core_ids=cores)
    x4 = [np.asarray(o["y"]) for o in r.results]

    out = np.concatenate(x4, axis=0).reshape(2, 8192, D).astype(np.float32)
    return out


if __name__ == "__main__":
    for nm, b in [("l0a", build_l0_attn), ("l0f", build_l0_ffn), ("l1s", build_l1_s4),
                  ("l1d", build_l1_dequant), ("l1f", build_l1_ffn)]:
        nc = b()
        print(nm, "instructions:", nc.n_instructions())
